# version 08 (final)
# baseline (speedup 1.0000x reference)
import contextlib
import os
import numpy as np
import ml_dtypes
import concourse.bass as bass
import concourse.mybir as mybir
from concourse.expressions_rust import RuntimeValue
from concourse.bass_utils import run_bass_kernel_spmd

F32 = mybir.dt.float32
BF16 = mybir.dt.bfloat16
I32 = mybir.dt.int32
AL = mybir.AluOpType
AF = mybir.ActivationFunctionType
DynS = bass.DynSlice

MAGIC = float(1.5 * 2 ** 23)

KSTOP = int(os.environ.get("KSTOP", "99"))
KDUMP = os.environ.get("KDUMP", "")
DM, DI, DS, E2, FF = 1024, 2048, 16, 2080, 4096
TL, TH = 2048, 4
HW = 128
TT, NT = 512, 4
SG, NG = 1024, 4
NCC = 8


def build_nc():
    nc = bass.Bass("TRN2", target_bir_lowering=False,
                   enable_partition_id=True, detect_race_conditions=False)

    x_ext = nc.dram_tensor("x_ext", [HW + TL, DM], F32, kind="ExternalInput").ap()
    wi_n = nc.dram_tensor("wi_n", [2 * DI, DM], F32, kind="ExternalInput").ap()
    wx_n = nc.dram_tensor("wx_n", [E2, DI], F32, kind="ExternalInput").ap()
    wd_n = nc.dram_tensor("wd_n", [DI, DI], F32, kind="ExternalInput").ap()
    wo_n = nc.dram_tensor("wo_n", [DM, DI], F32, kind="ExternalInput").ap()
    wg_n = nc.dram_tensor("wg_n", [FF, DM], F32, kind="ExternalInput").ap()
    wu_n = nc.dram_tensor("wu_n", [FF, DM], F32, kind="ExternalInput").ap()
    wdn_n = nc.dram_tensor("wdn_n", [DM, FF], F32, kind="ExternalInput").ap()
    cwr = nc.dram_tensor("cwr", [128, 64], F32, kind="ExternalInput").ap()
    cbr = nc.dram_tensor("cbr", [128, 16], F32, kind="ExternalInput").ap()
    dbr = nc.dram_tensor("dbr", [128, 16], F32, kind="ExternalInput").ap()
    alogr = nc.dram_tensor("alogr", [128, 128], F32, kind="ExternalInput").ap()
    dr = nc.dram_tensor("dr", [128, 8], F32, kind="ExternalInput").ap()
    identf = nc.dram_tensor("identf", [128, 128], F32, kind="ExternalInput").ap()
    identb = nc.dram_tensor("identb", [128, 128], BF16, kind="ExternalInput").ap()
    onescol = nc.dram_tensor("onescol", [128, 1], BF16, kind="ExternalInput").ap()
    onesrow = nc.dram_tensor("onesrow", [1, 128], BF16, kind="ExternalInput").ap()
    onesrowf = nc.dram_tensor("onesrowf", [1, 128], F32, kind="ExternalInput").ap()
    fconst = nc.dram_tensor("fconst", [2, 64], I32, kind="ExternalInput").ap()
    cvals = nc.dram_tensor("cvals", [128, 4], F32, kind="ExternalInput").ap()
    selbc = nc.dram_tensor("selbc", [32, 4096], BF16, kind="ExternalInput").ap()
    sel4 = nc.dram_tensor("sel4", [4, 512], F32, kind="ExternalInput").ap()

    out_h = nc.dram_tensor("o", [TL, DM], F32, kind="ExternalOutput").ap()

    park_x = nc.dram_tensor("park_x", [DM, HW + TL], F32, kind="Internal").ap()
    park_sz = nc.dram_tensor("park_sz", [DI, TL], BF16, kind="Internal").ap()
    park_xbl = nc.dram_tensor("park_xbl", [1024, TL], BF16, kind="Internal").ap()
    park_dl = nc.dram_tensor("park_dl", [1024, TL], BF16, kind="Internal").ap()
    park_bcl = nc.dram_tensor("park_bcl", [32, TL], F32, kind="Internal").ap()
    park_xbf = nc.dram_tensor("park_xbf", [DI, TL], BF16, kind="Internal").ap()
    park_din = nc.dram_tensor("park_din", [DI, TL], BF16, kind="Internal").ap()
    park_yz = nc.dram_tensor("park_yz", [1024, TL], BF16, kind="Internal").ap()
    park_xp = nc.dram_tensor("park_xp", [DM, TL], F32, kind="Internal").ap()
    park_g = nc.dram_tensor("park_g", [FF, TL], BF16, kind="Internal").ap()
    park_guf = nc.dram_tensor("park_guf", [FF, TL], F32, kind="Internal").ap()
    park_guq = nc.dram_tensor("park_guq", [FF, TL], BF16, kind="Internal").ap()
    park_xq = nc.dram_tensor("park_xq", [DM, TL], BF16, kind="Internal").ap()
    park_isx = nc.dram_tensor("park_isx", [128, TL], F32, kind="Internal").ap()
    park_is2 = nc.dram_tensor("park_is2", [128, TL], F32, kind="Internal").ap()

    shd_b = nc.dram_tensor("sh_d", [2048, TL], BF16, kind="Internal",
                           addr_space="Shared").ap()
    shxb_b = nc.dram_tensor("sh_xb", [2048, TL], BF16, kind="Internal",
                            addr_space="Shared").ap()
    shbc_f = nc.dram_tensor("sh_bc", [64, TL], F32, kind="Internal",
                            addr_space="Shared").ap()
    shy_b = nc.dram_tensor("sh_y", [2048, TL], BF16, kind="Internal",
                           addr_space="Shared").ap()
    sh_fl = nc.dram_tensor("sh_fl", [2, 64], I32, kind="Internal",
                           addr_space="Shared").ap()

    est = contextlib.ExitStack()
    sbt = lambda n, s, d: est.enter_context(nc.sbuf_tensor(n, s, d))
    pst = lambda n, s: est.enter_context(nc.psum_tensor(n, s, F32))
    semt = lambda n: est.enter_context(nc.semaphore(n))

    with est:
        fc = sbt("fc", [2, 64], I32)
        cpool = sbt("cpool", [128, 4], F32)

        fb = sbt("fb", [1, 64], I32)
        idf = sbt("idf", [128, 128], F32)
        idb = sbt("idb", [128, 128], BF16)
        onec = sbt("onec", [128, 1], BF16)
        oner = sbt("oner", [1, 128], BF16)
        onerf = sbt("onerf", [1, 128], F32)
        a_sb = sbt("a_sb", [128, 128], F32)
        cw_sb = sbt("cw_sb", [128, 64], F32)
        cb_sb = sbt("cb_sb", [128, 16], F32)
        db_sb = sbt("db_sb", [128, 16], F32)
        dv_sb = sbt("dv_sb", [128, 8], F32)
        wbuf = sbt("wbuf", [128, 33280], BF16)
        wnat = sbt("wnat", [128, 2048], F32)
        wnb = sbt("wnb", [128, 2048], BF16)
        tA = sbt("tA", [128, 4096], F32)
        tB = sbt("tB", [128, 4096], F32)
        tC = sbt("tC", [128, 1024], F32)
        tD = sbt("tD", [128, 1024], F32)
        bA = sbt("bA", [128, 8256], BF16)
        bB = sbt("bB", [128, 8192], BF16)
        bD = sbt("bD", [128, 8192], BF16)
        bE = sbt("bE", [128, 1024], BF16)
        hbuf = sbt("hbuf", [128, 1024], F32)
        ycb = sbt("ycb", [128, 2048], BF16)
        wsc = sbt("wsc", [128, 2048], BF16)
        u8 = sbt("u8", [128, 8192], BF16)
        rbcf = None
        mbcf = None
        invsx = None
        invs2 = None
        wsg = sbt("wsg", [128, 8], F32)
        bcw_xp = u8[0:32, 0:2048].bitcast(F32)
        bcw = bD[0:32, 2048:4096].bitcast(F32)
        bcb = tA[0:32, 512:1024].bitcast(BF16)
        xd1 = tA[:, 0:512].bitcast(BF16)
        stt = u8[0:1, 2048:5120].bitcast(F32)
        selb = bD[0:32, 4096:8192]
        sel4f = u8[0:4, 5120:6144].bitcast(F32)
        rbcf = u8[:, 0:1024].bitcast(F32)
        mbcf = u8[:, 1024:2048].bitcast(F32)
        isx_t = bA[:, 0:1024].bitcast(F32)
        is2_t = bA[:, 1024:2048].bitcast(F32)
        state = sbt("state", [128, 128], F32)

        ps_a = pst("ps_a", [128, 512])
        ps_b = pst("ps_b", [128, 512])
        ps_t = pst("ps_t", [128, 512])
        ps_y = pst("ps_y", [128, 1024])
        ps_r = pst("ps_r", [128, 1024])
        ps_s = pst("ps_s", [128, 512])

        sq = semt("sq"); gq = semt("gq"); cq = semt("cq"); pq = semt("pq")
        tc = semt("tc"); vc = semt("vc"); sc = semt("sc"); gc = semt("gc")
        bar = semt("bar")

        K = dict(sq=0, gq=0, cq=0, tc=0, vc=0, sc=0, gc=0, bar=0)
        SEMS = dict(sq=sq, gq=gq, cq=cq, tc=tc, vc=vc, sc=sc, gc=gc, bar=bar)
        prog = {k: [] for k in ("sync", "gpsimd", "vector", "scalar", "tensor")}
        last_inst = {k: None for k in prog}

        def em(eng, fn):
            def wrap(e, fn=fn, eng=eng):
                r = fn(e)
                if r is not None:
                    last_inst[eng] = r
            prog[eng].append(wrap)

        def dma(q, out_ap, in_ap):
            semh = {"sync": sq, "gpsimd": gq, "scalar": cq}[q]
            kk = {"sync": "sq", "gpsimd": "gq", "scalar": "cq"}[q]
            K[kk] += 16

            def do(e, o=out_ap, i=in_ap, s2=semh):
                try:
                    e.dma_start(out=o, in_=i).then_inc(s2, 16)
                except Exception:
                    print("DMA FAIL out=", o, " in=", i)
                    raise
            prog[q].append(do)
            return K[kk]

        def dma_lazy(q, mk):
            semh = {"sync": sq, "gpsimd": gq, "scalar": cq}[q]
            kk = {"sync": "sq", "gpsimd": "gq", "scalar": "cq"}[q]
            K[kk] += 16
            def do(e, mk=mk, s2=semh):
                o, i = mk(e)
                try:
                    e.dma_start(out=o, in_=i).then_inc(s2, 16)
                except Exception:
                    print("DMA_LAZY FAIL out=", o, " in=", i)
                    raise
            prog[q].append(do)
            return K[kk]

        def wait(eng, kk, val):
            if val <= 0:
                return
            em(eng, lambda e, s2=SEMS[kk], v=val: e.wait_ge(s2, v))

        OWN = dict(sync="sq", gpsimd="gc", vector="vc", scalar="sc",
                   tensor="tc")

        def bump(eng, kk):
            K[kk] += 1
            own = OWN[eng]
            ownv = K[own] - (1 if own == kk else 0)

            def do(e, s2=SEMS[kk], eng=eng, own=own, ownv=ownv):
                li = last_inst[eng]
                if li is not None:
                    li.then_inc(s2, 1)
                    last_inst[eng] = None
                else:
                    if ownv > 0:
                        e.wait_ge(SEMS[own], ownv)
                    e.sem_inc(s2, 1)
            prog[eng].append(do)
            return K[kk]

        def vsync():
            v = bump("vector", "vc")
            wait("vector", "vc", v)

        QKEY = {"sync": "sq", "gpsimd": "gq", "scalar": "cq"}

        def barrier():
            K["bar"] += 5
            tgt = K["bar"]
            for eng in prog:
                if eng in QKEY:
                    wait(eng, QKEY[eng], K[QKEY[eng]])
                own = OWN[eng]
                ownv = K[own]

                def do(e, eng=eng, own=own, ownv=ownv):
                    li = last_inst[eng]
                    if li is not None:
                        li.then_inc(bar, 1)
                        last_inst[eng] = None
                    else:
                        if ownv > 0:
                            e.wait_ge(SEMS[own], ownv)
                        e.sem_inc(bar, 1)
                prog[eng].append(do)
                em(eng, lambda e, v=tgt: e.wait_ge(bar, v))

        regs = {}

        def g_init(g):
            pid = g.alloc_register("pid")
            g.reg_load(pid, nc.partition_id_tensor.ap()[0:1, 0:1])
            me = g.alloc_register("me")
            g.reg_alu(me, pid, 1, AL.bitwise_and)
            ot = g.alloc_register("ot")
            g.reg_alu(ot, me, 1, AL.bitwise_xor)
            cnt = g.alloc_register("cnt")
            g.reg_mov(cnt, 0)
            regs["me_r"] = me
            regs["me"] = RuntimeValue(me, min_val=0, max_val=1)
            regs["ot"] = RuntimeValue(ot, min_val=0, max_val=1)
            regs["cnt"] = cnt
            regs["cnt_v"] = RuntimeValue(cnt, min_val=0)
            regs["hs"] = g.alloc_register("hs")
            regs["fv"] = g.alloc_register("fv")
            regs["own"] = g.alloc_register("own")
            regs["dr"] = g.alloc_register("dr")
            regs["gidx"] = g.alloc_register("gidx")
            me_row = g.alloc_register("me_row")
            g.reg_alu(me_row, regs["me_r"], 1024, AL.mult)
            ot_row = g.alloc_register("ot_row")
            g.reg_alu(ot_row, regs["ot"].val, 1024, AL.mult)
            me_bc = g.alloc_register("me_bc")
            g.reg_alu(me_bc, regs["me_r"], 32, AL.mult)
            ot_bc = g.alloc_register("ot_bc")
            g.reg_alu(ot_bc, regs["ot"].val, 32, AL.mult)
            regs["me_row"] = RuntimeValue(me_row, min_val=0, max_val=1024,
                                          guaranteed_mod_val=1024)
            regs["ot_row"] = RuntimeValue(ot_row, min_val=0, max_val=1024,
                                          guaranteed_mod_val=1024)
            regs["me_bc"] = RuntimeValue(me_bc, min_val=0, max_val=32,
                                         guaranteed_mod_val=32)
            regs["ot_bc"] = RuntimeValue(ot_bc, min_val=0, max_val=32,
                                         guaranteed_mod_val=32)
            regs["inbox"] = sh_fl[DynS(regs["me"], 1), :]
            regs["outbox"] = sh_fl[DynS(regs["ot"], 1), :]
            regs["code"] = g.alloc_register("code")
        em("gpsimd", g_init)

        def g_polldma(g, out_ap, in_ap):
            g.reg_add(regs["cnt"], regs["cnt"], 16)
            g.dma_start(out=out_ap, in_=in_ap).then_inc(pq, 16)
            g.wait_ge(pq, regs["cnt_v"])

        def g_poll_flag(g, idx):
            g.reg_mov(regs["fv"], 0)
            fv_v = RuntimeValue(regs["fv"], min_val=0, max_val=2)
            with g.While(fv_v < 1):
                g_polldma(g, fb[0:1, 0:64], regs["inbox"])
                if isinstance(idx, int):
                    g.reg_load(regs["fv"], fb[0:1, idx:idx + 1])
                else:
                    g.reg_load(regs["fv"], fb[0:1, DynS(idx, 1)])

        if KSTOP >= 0:
          c_consts = 0
          nc.const_aps.register(F32, 1e-5, cpool[:, 0:1])
          nc.const_aps.register(F32, 1.0, cpool[:, 1:2])
          nc.const_aps.register(F32, -1.0, cpool[:, 2:3])
          nc.const_aps.register(F32, 0.0, cpool[:, 3:4])
          for (dst, src) in ((fc[:], fconst), (cpool[:], cvals),
                             (idf[:], identf), (idb[:], identb),
                             (onec[:], onescol), (oner[:], onesrow), (onerf[:], onesrowf),
                             (a_sb[:], alogr), (cw_sb[:], cwr), (cb_sb[:], cbr),
                             (db_sb[:], dbr), (dv_sb[:], dr)):
              c_consts = dma("sync", dst, src)

          def g_hs(g):
              g.wait_ge(sq, c_consts)
              g_polldma(g, regs["inbox"], fc[0:1, 0:64])
              g.reg_mov(regs["hs"], 0)
              hs_v = RuntimeValue(regs["hs"], min_val=0, max_val=2)
              with g.While(hs_v < 1):
                  g_polldma(g, regs["outbox"][0:1, 63:64], fc[1:2, 1:2])
                  g_polldma(g, fb[0:1, 0:64], regs["inbox"])
                  g.reg_load(regs["hs"], fb[0:1, 63:64])
          em("gpsimd", g_hs)
          hs_done = bump("gpsimd", "gc")

          wait("scalar", "sq", c_consts)
          em("scalar", lambda e: e.activation(out=tA[:, 0:128], in_=a_sb[:],
                                              func=AF.Exp))
          sA = bump("scalar", "sc")
          wait("vector", "sc", sA)
          em("vector", lambda e: e.tensor_scalar_mul(out=a_sb[:], in0=tA[:, 0:128],
                                                     scalar1=-1.0))
          bump("vector", "vc")

          def wtrans(w_src, rows, cols, colw, quant=None):
              n_ec = (rows + 127) // 128
              last_tc = 0
              for ec in range(n_ec):
                  r0 = ec * 128
                  pr = min(128, rows - r0)
                  for c0 in range(0, cols, 2048):
                      cw_ = min(2048, cols - c0)
                      c_l = dma("scalar", wnat[0:pr, 0:cw_],
                                w_src[r0:r0 + pr, c0:c0 + cw_])
                      wait("vector", "cq", c_l)
                      wait("vector", "tc", last_tc)
                      if quant is None:
                          em("vector", lambda e, pr=pr, cw_=cw_: e.tensor_copy(
                              out=wnb[0:pr, 0:cw_], in_=wnat[0:pr, 0:cw_]))
                      else:
                          em("vector", lambda e, pr=pr, cw_=cw_, q=quant:
                             e.tensor_scalar(
                              out=tB[0:pr, 0:cw_], in0=wnat[0:pr, 0:cw_],
                              scalar1=wsg[0:pr, q:q + 1], scalar2=1.0,
                              op0=AL.mult, op1=AL.min))
                          em("vector", lambda e, pr=pr, cw_=cw_: e.tensor_scalar(
                              out=tB[0:pr, 0:cw_], in0=tB[0:pr, 0:cw_],
                              scalar1=-1.0, scalar2=MAGIC, op0=AL.max,
                              op1=AL.add))
                          em("vector", lambda e, pr=pr, cw_=cw_: e.tensor_scalar(
                              out=wnb[0:pr, 0:cw_], in0=tB[0:pr, 0:cw_],
                              scalar1=MAGIC, scalar2=None, op0=AL.subtract))
                      v_c = bump("vector", "vc")
                      wait("tensor", "vc", v_c)
                      nkc = cw_ // 128
                      for k4 in range(0, nkc, 4):
                          kn = min(4, nkc - k4)
                          for j in range(kn):
                              kc = k4 + j
                              em("tensor", lambda e, kc=kc, j=j, pr=pr: e.matmul(
                                  out=ps_t[0:128, j * 128:j * 128 + pr],
                                  lhsT=wnb[0:pr, kc * 128:(kc + 1) * 128],
                                  rhs=idb[0:pr, 0:pr], start=True, stop=True))
                          t_c = bump("tensor", "tc")
                          wait("scalar", "tc", t_c)
                          for j in range(kn):
                              kc = k4 + j
                              off = (kc + c0 // 128) * colw + r0
                              em("scalar", lambda e, off=off, j=j, pr=pr: e.copy(
                                  out=wbuf[:, off:off + pr],
                                  in_=ps_t[:, j * 128:j * 128 + pr]))
                          s_c = bump("scalar", "sc")
                          wait("tensor", "sc", s_c)
                      last_tc = K["tc"]
                      wait("scalar", "vc", v_c)

          def layernorm(src, dst_b, dst_f, tw):
              for kc in range(8):
                  wait("vector", "tc", K["tc"])
                  em("vector", lambda e, kc=kc: e.tensor_copy(
                      out=bE[:, 0:tw], in_=src[:, kc * tw:(kc + 1) * tw]))
                  v_c = bump("vector", "vc")
                  wait("tensor", "vc", v_c)
                  em("tensor", lambda e, kc=kc: e.matmul(
                      out=ps_s[0:1, 0:tw], lhsT=onec[:, 0:1], rhs=bE[:, 0:tw],
                      start=(kc == 0), stop=(kc == 7)))
                  bump("tensor", "tc")
              t_m = K["tc"]
              wait("vector", "tc", t_m)
              em("vector", lambda e: e.tensor_scalar_mul(
                  out=stt[0:1, 0:tw], in0=ps_s[0:1, 0:tw], scalar1=1.0 / DM))
              v_m = bump("vector", "vc")
              wait("scalar", "tc", t_m)
              for kc in range(8):
                  em("scalar", lambda e, kc=kc: e.activation(
                      out=bE[:, 0:tw], in_=src[:, kc * tw:(kc + 1) * tw],
                      func=AF.Square))
                  s_c = bump("scalar", "sc")
                  wait("tensor", "sc", s_c)
                  if kc == 0:
                      wait("tensor", "vc", v_m)
                  em("tensor", lambda e, kc=kc: e.matmul(
                      out=ps_s[0:1, 0:tw], lhsT=onec[:, 0:1], rhs=bE[:, 0:tw],
                      start=(kc == 0), stop=(kc == 7)))
                  t_c = bump("tensor", "tc")
                  wait("scalar", "tc", t_c)
              t_m2 = K["tc"]
              wait("vector", "tc", t_m2)
              em("vector", lambda e: e.tensor_scalar_mul(
                  out=stt[0:1, 512:512 + tw], in0=ps_s[0:1, 0:tw],
                  scalar1=1.0 / DM))
              em("vector", lambda e: e.scalar_tensor_tensor(
                  out=stt[0:1, 1024:1024 + tw], in0=stt[0:1, 0:tw], scalar=-1.0,
                  in1=stt[0:1, 0:tw], op0=AL.mult, op1=AL.mult))
              em("vector", lambda e: e.tensor_add(
                  out=stt[0:1, 1024:1024 + tw], in0=stt[0:1, 1024:1024 + tw],
                  in1=stt[0:1, 512:512 + tw]))
              em("vector", lambda e: e.tensor_scalar_max(
                  out=stt[0:1, 1024:1024 + tw], in0=stt[0:1, 1024:1024 + tw],
                  scalar1=0.0))
              v_v = bump("vector", "vc")
              wait("scalar", "vc", v_v)
              em("scalar", lambda e: e.activation(
                  out=stt[0:1, 512:512 + tw], in_=stt[0:1, 1024:1024 + tw],
                  func=AF.Sqrt, bias=1e-5))
              s_sd = bump("scalar", "sc")
              wait("vector", "sc", s_sd)
              em("vector", lambda e: e.reciprocal(
                  out=stt[0:1, 1024:1024 + tw], in_=stt[0:1, 512:512 + tw]))
              em("vector", lambda e: e.tensor_mul(
                  out=stt[0:1, 512:512 + tw], in0=stt[0:1, 0:tw],
                  in1=stt[0:1, 1024:1024 + tw]))
              v_st = bump("vector", "vc")
              wait("tensor", "vc", v_st)
              em("tensor", lambda e: e.matmul(
                  out=ps_s[0:128, 0:tw], lhsT=onerf[0:1, :],
                  rhs=stt[0:1, 1024:1024 + tw], start=True, stop=True))
              t_r = bump("tensor", "tc")
              wait("scalar", "tc", t_r)
              em("scalar", lambda e: e.copy(out=rbcf[:, 0:tw], in_=ps_s[:, 0:tw]))
              s_r = bump("scalar", "sc")
              wait("tensor", "sc", s_r)
              em("tensor", lambda e: e.matmul(
                  out=ps_s[0:128, 0:tw], lhsT=onerf[0:1, :],
                  rhs=stt[0:1, 512:512 + tw], start=True, stop=True))
              t_r2 = bump("tensor", "tc")
              wait("scalar", "tc", t_r2)
              em("scalar", lambda e: e.copy(out=mbcf[:, 0:tw], in_=ps_s[:, 0:tw]))
              s_r2 = bump("scalar", "sc")
              wait("vector", "sc", s_r2)
              for kc in range(8):
                  em("vector", lambda e, kc=kc: e.tensor_mul(
                      out=tB[:, kc * tw:(kc + 1) * tw],
                      in0=src[:, kc * tw:(kc + 1) * tw], in1=rbcf[:, 0:tw]))
                  em("vector", lambda e, kc=kc: e.tensor_sub(
                      out=dst_b[:, kc * tw:(kc + 1) * tw],
                      in0=tB[:, kc * tw:(kc + 1) * tw], in1=mbcf[:, 0:tw]))
                  if dst_f is not None:
                      em("vector", lambda e, kc=kc: e.tensor_sub(
                          out=dst_f[:, kc * tw:(kc + 1) * tw],
                          in0=tB[:, kc * tw:(kc + 1) * tw], in1=mbcf[:, 0:tw]))
              return bump("vector", "vc")

          def token_max_bcast(src_f, nchunks, tw, out_bcast):
              v_prev_abs = K["vc"]
              for kc in range(nchunks):
                  half = (kc % 2) * 512
                  wait("scalar", "vc", v_prev_abs)
                  em("scalar", lambda e, kc=kc, half=half: e.activation(
                      out=bE[:, half:half + tw],
                      in_=src_f[:, kc * tw:(kc + 1) * tw], func=AF.Abs))
                  s_a = bump("scalar", "sc")
                  wait("vector", "sc", s_a)
                  if kc == 0:
                      em("vector", lambda e, half=half: e.tensor_copy(
                          out=tC[:, 0:tw], in_=bE[:, half:half + tw]))
                  else:
                      em("vector", lambda e, half=half: e.tensor_max(
                          out=tC[:, 0:tw], in0=tC[:, 0:tw],
                          in1=bE[:, half:half + tw]))
                  v_prev_abs = bump("vector", "vc")
              v_mx = v_prev_abs
              wait("tensor", "vc", v_mx)
              for j in range(4):
                  em("tensor", lambda e, j=j: e.transpose(
                      out=ps_t[0:128, j * 128:(j + 1) * 128],
                      in_=tC[:, j * 128:(j + 1) * 128], identity=idf[:, :]))
              t_t = bump("tensor", "tc")
              wait("vector", "tc", t_t)
              for j in range(4):
                  em("vector", lambda e, j=j: e.tensor_reduce(
                      out=tD[:, j:j + 1], in_=ps_t[:, j * 128:(j + 1) * 128],
                      axis=mybir.AxisListType.X, op=AL.max))
              v_r = bump("vector", "vc")
              wait("tensor", "vc", v_r)
              em("tensor", lambda e: e.transpose(
                  out=ps_s[0:4, 0:128], in_=tD[:, 0:4], identity=idf[:, :]))
              t_t2 = bump("tensor", "tc")
              wait("scalar", "tc", t_t2)
              em("scalar", lambda e: e.copy(out=tD[0:4, 4:132],
                                            in_=ps_s[0:4, 0:128]))
              s_c = bump("scalar", "sc")
              wait("vector", "sc", s_c)
              em("vector", lambda e: e.tensor_scalar_max(
                  out=tD[0:4, 4:132], in0=tD[0:4, 4:132], scalar1=1e-5))
              v_b = bump("vector", "vc")
              wait("tensor", "vc", v_b)
              for j in range(4):
                  em("tensor", lambda e, j=j: e.matmul(
                      out=ps_s[0:128, j * 128:(j + 1) * 128],
                      lhsT=sel4f[0:4, j * 128:(j + 1) * 128],
                      rhs=tD[0:4, 4:132], start=True, stop=True))
              t_b = bump("tensor", "tc")
              wait("scalar", "tc", t_b)
              em("scalar", lambda e: e.copy(out=out_bcast[:, 0:tw],
                                            in_=ps_s[:, 0:tw]))
              return bump("scalar", "sc")

          wait("vector", "gc", hs_done)
          wtrans(wi_n, 2 * DI, DM, 4096)
          barrier()

        if KSTOP >= 0:
          def phase1a(tau):
              if tau < 0:
                  t0, tw = 0, HW
              else:
                  t0, tw = HW + tau * TT, TT
              nts = (tw + 127) // 128
              for tsub in range(nts):
                  a0 = t0 + tsub * 128
                  aw = min(128, t0 + tw - a0)
                  c_l = dma("sync", tB[0:aw, 0:DM], x_ext[a0:a0 + aw, :])
                  wait("tensor", "sq", c_l)
                  for d4 in range(0, 8, 4):
                      for j in range(4):
                          dc = d4 + j
                          em("tensor", lambda e, j=j, dc=dc, aw=aw: e.transpose(
                              out=ps_t[0:128, j * 128:j * 128 + aw],
                              in_=tB[0:aw, dc * 128:(dc + 1) * 128],
                              identity=idf[0:aw, 0:aw]))
                      t_c = bump("tensor", "tc")
                      wait("scalar", "tc", t_c)
                      for j in range(4):
                          dc = d4 + j
                          em("scalar", lambda e, j=j, dc=dc, aw=aw, tsub=tsub,
                             tw=tw: e.copy(
                              out=tA[:, dc * tw + tsub * 128:
                                     dc * tw + tsub * 128 + aw],
                              in_=ps_t[:, j * 128:j * 128 + aw]))
                      s_c = bump("scalar", "sc")
                      wait("tensor", "sc", s_c)
                  s_c = K["sc"]
                  wait("sync", "sc", s_c)
              s_last = K["sc"]
              wait("sync", "sc", s_last)
              for dc in range(8):
                  dma("sync", park_x[dc * 128:(dc + 1) * 128, t0:t0 + tw],
                      tA[:, dc * tw:dc * tw + tw])
              wait("vector", "sc", s_last)
              v_ln = layernorm(tA, bD, None, tw)
              wait("tensor", "vc", v_ln)
              n_ec = 32 if tau >= 0 else 16
              for ec in range(n_ec):
                  bank = ps_a if (ec % 2 == 0) else ps_b
                  for kc in range(8):
                      em("tensor", lambda e, ec=ec, kc=kc, bank=bank, tw=tw:
                         e.matmul(
                          out=bank[:, 0:tw],
                          lhsT=wbuf[:, kc * 4096 + ec * 128:
                                    kc * 4096 + ec * 128 + 128],
                          rhs=bD[:, kc * tw:(kc + 1) * tw],
                          start=(kc == 0), stop=(kc == 7)))
                  t_mm = bump("tensor", "tc")
                  wait("scalar", "tc", t_mm)
                  if ec < 16:
                      if tau >= 0:
                          off = ec * 516 + 4
                          em("scalar", lambda e, off=off, bank=bank, tw=tw:
                             e.copy(out=bA[:, off:off + tw],
                                    in_=bank[:, 0:tw]))
                      else:
                          off = ec * 516
                          em("scalar", lambda e, off=off, bank=bank, tw=tw:
                             e.copy(out=bA[:, off:off + 4],
                                    in_=bank[:, tw - 4:tw]))
                      s_e = bump("scalar", "sc")
                      wait("tensor", "sc", s_e)
                  else:
                      em("scalar", lambda e, bank=bank, tw=tw: e.activation(
                          out=bE[:, 0:tw], in_=bank[:, 0:tw], func=AF.Silu))
                      s_e = bump("scalar", "sc")
                      wait("tensor", "sc", s_e)
                      wait("sync", "sc", s_e)
                      c_s = dma("sync",
                                park_sz[(ec - 16) * 128:(ec - 15) * 128,
                                        tau * TT:tau * TT + tw], bE[:, 0:tw])
                      wait("scalar", "sq", c_s)
              if tau < 0:
                  return
              s_ip = K["sc"]
              wait("vector", "sc", s_ip)
              for cc in range(16):
                  em("vector", lambda e, cc=cc: e.tensor_scalar_mul(
                      out=tC[:, 0:TT], in0=bA[:, cc * 516 + 1:cc * 516 + 1 + TT],
                      scalar1=cw_sb[:, cc * 4:cc * 4 + 1]))
                  for k in range(1, 4):
                      em("vector", lambda e, cc=cc, k=k: e.scalar_tensor_tensor(
                          out=tC[:, 0:TT],
                          in0=bA[:, cc * 516 + 1 + k:cc * 516 + 1 + k + TT],
                          scalar=cw_sb[:, cc * 4 + k:cc * 4 + k + 1],
                          in1=tC[:, 0:TT], op0=AL.mult, op1=AL.add))
                  v_cv = bump("vector", "vc")
                  wait("scalar", "vc", v_cv)
                  em("scalar", lambda e, cc=cc: e.activation(
                      out=bB[:, cc * TT:(cc + 1) * TT], in_=tC[:, 0:TT],
                      func=AF.Silu, bias=cb_sb[:, cc:cc + 1]))
                  s_cv = bump("scalar", "sc")
                  wait("vector", "sc", s_cv)
              s_conv = K["sc"]
              wait("vector", "sc", s_conv)
              em("vector", lambda e: e.tensor_copy(
                  out=bA[:, 0:8256].rearrange("p (c w) -> p c w", w=516)[:, :, 0:4],
                  in_=bA[:, 0:8256].rearrange("p (c w) -> p c w", w=516)[:, :, 512:516]))
              v_tail = bump("vector", "vc")
              wait("sync", "sc", s_conv)
              for cc in range(16):
                  dma("sync", park_xbf[cc * 128:(cc + 1) * 128,
                                       tau * TT:(tau + 1) * TT],
                      bB[:, cc * TT:(cc + 1) * TT])
              for cc in range(8):
                  dma("sync", park_xbl[cc * 128:(cc + 1) * 128,
                                       tau * TT:(tau + 1) * TT],
                      bB[:, cc * TT:(cc + 1) * TT])
              wait("gpsimd", "sc", s_conv)

              def g_sendxb(g, tau=tau):
                  for slot in (0, 1):
                      with g.If_eq(regs["me_r"], slot):
                          for cc in range(8, 16):
                              g.dma_start(
                                  out=shxb_b[slot * 1024 + (cc - 8) * 128:
                                             slot * 1024 + (cc - 7) * 128,
                                             tau * TT:(tau + 1) * TT],
                                  in_=bB[:, cc * TT:(cc + 1) * TT]).then_inc(gq, 16)
              em("gpsimd", g_sendxb)
              K["gq"] += 16 * 8

          phase1a(-1)
          barrier()
          for tau in range(NT):
              phase1a(tau)
              barrier()

        if KSTOP >= 2:
          wtrans(wx_n, E2, DI, 2080)
          barrier()
          for tau in range(NT):
              cl = 0
              for cc in range(16):
                  cl = dma("sync", bB[:, cc * TT:(cc + 1) * TT],
                           park_xbf[cc * 128:(cc + 1) * 128,
                                    tau * TT:(tau + 1) * TT])
              wait("tensor", "sq", cl)
              for ec in range(16):
                  bank = ps_a if (ec % 2 == 0) else ps_b
                  for kc in range(16):
                      em("tensor", lambda e, ec=ec, kc=kc, bank=bank: e.matmul(
                          out=bank[:, 0:TT],
                          lhsT=wbuf[:, kc * 2080 + ec * 128:
                                    kc * 2080 + ec * 128 + 128],
                          rhs=bB[:, kc * TT:(kc + 1) * TT],
                          start=(kc == 0), stop=(kc == 15)))
                  t_mm = bump("tensor", "tc")
                  wait("scalar", "tc", t_mm)
                  em("scalar", lambda e, ec=ec, bank=bank: e.copy(
                      out=bD[:, ec * TT:(ec + 1) * TT], in_=bank[:, 0:TT]))
                  s_e = bump("scalar", "sc")
                  wait("tensor", "sc", s_e)
              for kc in range(16):
                  em("tensor", lambda e, kc=kc: e.matmul(
                      out=ps_a[0:32, 0:TT],
                      lhsT=wbuf[:, kc * 2080 + 2048:kc * 2080 + 2080],
                      rhs=bB[:, kc * TT:(kc + 1) * TT],
                      start=(kc == 0), stop=(kc == 15)))
              t_bc = bump("tensor", "tc")
              wait("vector", "tc", t_bc)
              em("vector", lambda e: e.tensor_copy(out=bcw_xp[0:32, 0:TT],
                                                   in_=ps_a[0:32, 0:TT]))
              v_bc = bump("vector", "vc")
              wait("tensor", "vc", v_bc)
              s_all = K["sc"]
              wait("sync", "sc", s_all)
              for cc in range(16):
                  dma("sync", park_din[cc * 128:(cc + 1) * 128,
                                       tau * TT:(tau + 1) * TT],
                      bD[:, cc * TT:(cc + 1) * TT])
              wait("sync", "vc", v_bc)
              dma("sync", park_bcl[:, tau * TT:(tau + 1) * TT],
                  bcw_xp[0:32, 0:TT])
              wait("gpsimd", "vc", v_bc)

              def g_sendbc(g, tau=tau):
                  for slot in (0, 1):
                      with g.If_eq(regs["me_r"], slot):
                          g.dma_start(
                              out=shbc_f[slot * 32:slot * 32 + 32,
                                         tau * TT:(tau + 1) * TT],
                              in_=bcw_xp[0:32, 0:TT]).then_inc(gq, 16)
              em("gpsimd", g_sendbc)
              K["gq"] += 16
              barrier()

        if KSTOP >= 3:
          wtrans(wd_n, DI, DI, 2048)
          barrier()
          for tau in range(NT):
              cl = 0
              for cc in range(16):
                  cl = dma("sync", bB[:, cc * TT:(cc + 1) * TT],
                           park_din[cc * 128:(cc + 1) * 128,
                                    tau * TT:(tau + 1) * TT])
              wait("tensor", "sq", cl)
              for ec in range(16):
                  bank = ps_a if (ec % 2 == 0) else ps_b
                  for kc in range(16):
                      em("tensor", lambda e, ec=ec, kc=kc, bank=bank: e.matmul(
                          out=bank[:, 0:TT],
                          lhsT=wbuf[:, kc * 2048 + ec * 128:
                                    kc * 2048 + ec * 128 + 128],
                          rhs=bB[:, kc * TT:(kc + 1) * TT],
                          start=(kc == 0), stop=(kc == 15)))
                  t_mm = bump("tensor", "tc")
                  wait("scalar", "tc", t_mm)
                  em("scalar", lambda e, ec=ec, bank=bank: e.activation(
                      out=tC[:, 0:TT], in_=bank[:, 0:TT], func=AF.Abs,
                      bias=db_sb[:, ec:ec + 1]))
                  em("scalar", lambda e: e.activation(
                      out=tC[:, 0:TT], in_=tC[:, 0:TT], func=AF.Exp, scale=-1.0))
                  em("scalar", lambda e: e.activation(
                      out=tC[:, 0:TT], in_=tC[:, 0:TT], func=AF.Ln, bias=1.0))
                  em("scalar", lambda e, ec=ec, bank=bank: e.activation(
                      out=tD[:, 0:TT], in_=bank[:, 0:TT], func=AF.Relu,
                      bias=db_sb[:, ec:ec + 1]))
                  s_sp = bump("scalar", "sc")
                  wait("vector", "sc", s_sp)
                  em("vector", lambda e, ec=ec: e.tensor_add(
                      out=bD[:, ec * TT:(ec + 1) * TT], in0=tC[:, 0:TT],
                      in1=tD[:, 0:TT]))
                  v_d = bump("vector", "vc")
                  wait("tensor", "vc", v_d)
                  wait("scalar", "vc", v_d)
              v_all = K["vc"]
              wait("sync", "vc", v_all)
              for cc in range(8):
                  dma("sync", park_dl[cc * 128:(cc + 1) * 128,
                                      tau * TT:(tau + 1) * TT],
                      bD[:, cc * TT:(cc + 1) * TT])
              wait("gpsimd", "vc", v_all)

              def g_sendd(g, tau=tau):
                  for slot in (0, 1):
                      with g.If_eq(regs["me_r"], slot):
                          for cc in range(8, 16):
                              g.dma_start(
                                  out=shd_b[slot * 1024 + (cc - 8) * 128:
                                            slot * 1024 + (cc - 7) * 128,
                                            tau * TT:(tau + 1) * TT],
                                  in_=bD[:, cc * TT:(cc + 1) * TT]).then_inc(gq, 16)
              em("gpsimd", g_sendd)
              K["gq"] += 16 * 8
              gq_t, sq_t = K["gq"], K["sq"]

              def g_flag(g, gq_t=gq_t, sq_t=sq_t, tau=tau):
                  g.wait_ge(gq, gq_t)
                  g.wait_ge(sq, sq_t)
                  g_polldma(g, regs["outbox"][0:1, tau:tau + 1], fc[1:2, 1:2])
              em("gpsimd", g_flag)
              barrier()

        if KSTOP >= 4:
          c_selb = dma("sync", selb, selbc)
          wait("tensor", "sq", c_selb)
          em("vector", lambda e: e.memset(state[:], 0.0))
          v_st0 = bump("vector", "vc")
          for g2 in range(NG):
              qown = g2 // 2
              jloc = (g2 % 2) * 2
              tloc = jloc * TT

              def g_ph2(g, g2=g2, qown=qown, jloc=jloc, tloc=tloc):
                  g.reg_mov(regs["own"], qown)
                  g.reg_alu(regs["dr"], regs["own"], regs["me_r"], AL.bitwise_xor)
                  g.reg_alu(regs["code"], regs["dr"], 2, AL.mult)
                  g.reg_alu(regs["code"], regs["code"], regs["me_r"], AL.add)
                  dr_v = RuntimeValue(regs["dr"], min_val=0, max_val=1)
                  with g.If(dr_v >= 1):
                      g_poll_flag(g, jloc)
                      g_poll_flag(g, jloc + 1)
                  with g.If(dr_v < 1):
                      for cc in range(8):
                          g.dma_start(
                              out=bA[:, cc * SG:(cc + 1) * SG],
                              in_=park_dl[cc * 128:(cc + 1) * 128,
                                          tloc:tloc + SG]).then_inc(gq, 16)
                          g.dma_start(
                              out=bB[:, cc * SG:(cc + 1) * SG],
                              in_=park_xbl[cc * 128:(cc + 1) * 128,
                                           tloc:tloc + SG]).then_inc(gq, 16)
                      g.dma_start(out=bcw[0:32, 0:SG],
                                  in_=park_bcl[:, tloc:tloc + SG]).then_inc(gq, 16)
                  for code, slot in ((2, 1), (3, 0)):
                      with g.If_eq(regs["code"], code):
                          for cc in range(8):
                              g.dma_start(
                                  out=bA[:, cc * SG:(cc + 1) * SG],
                                  in_=shd_b[slot * 1024 + cc * 128:
                                            slot * 1024 + (cc + 1) * 128,
                                            tloc:tloc + SG]).then_inc(gq, 16)
                              g.dma_start(
                                  out=bB[:, cc * SG:(cc + 1) * SG],
                                  in_=shxb_b[slot * 1024 + cc * 128:
                                             slot * 1024 + (cc + 1) * 128,
                                             tloc:tloc + SG]).then_inc(gq, 16)
                          g.dma_start(out=bcw[0:32, 0:SG],
                                      in_=shbc_f[slot * 32:slot * 32 + 32,
                                                 tloc:tloc + SG]).then_inc(gq, 16)
              em("gpsimd", g_ph2)
              K["gq"] += 16 * 17
              gq_ld = K["gq"]
              wait("vector", "gq", gq_ld)
              for cc in range(8):
                  em("vector", lambda e, cc=cc: e.tensor_mul(
                      out=u8[:, cc * SG:(cc + 1) * SG],
                      in0=bA[:, cc * SG:(cc + 1) * SG],
                      in1=bB[:, cc * SG:(cc + 1) * SG]))
              em("vector", lambda e: e.tensor_copy(out=bcb[0:32, 0:SG],
                                                   in_=bcw[0:32, 0:SG]))
              v_u = bump("vector", "vc")
              wait("tensor", "vc", v_u)
              for s in range(DS):
                  for jh in range(2):
                      em("tensor", lambda e, s=s, jh=jh: e.matmul(
                          out=ps_r[:, jh * 512:(jh + 1) * 512],
                          lhsT=selb[0:32, s * 128:(s + 1) * 128],
                          rhs=bcb[0:32, jh * 512:(jh + 1) * 512],
                          start=True, stop=True))
                  t_b = bump("tensor", "tc")
                  wait("scalar", "tc", t_b)
                  em("scalar", lambda e, s=s: e.copy(
                      out=wbuf[:, s * SG:(s + 1) * SG], in_=ps_r[:, 0:SG]))
                  s_b = bump("scalar", "sc")
                  wait("tensor", "sc", s_b)
                  for jh in range(2):
                      em("tensor", lambda e, s=s, jh=jh: e.matmul(
                          out=ps_r[:, jh * 512:(jh + 1) * 512],
                          lhsT=selb[0:32, (16 + s) * 128:(17 + s) * 128],
                          rhs=bcb[0:32, jh * 512:(jh + 1) * 512],
                          start=True, stop=True))
                  t_b2 = bump("tensor", "tc")
                  wait("scalar", "tc", t_b2)
                  em("scalar", lambda e, s=s: e.copy(
                      out=wbuf[:, 16384 + s * SG:16384 + (s + 1) * SG],
                      in_=ps_r[:, 0:SG]))
                  s_b2 = bump("scalar", "sc")
                  wait("tensor", "sc", s_b2)
              s_bc_done = K["sc"]
              wait("gpsimd", "sc", s_bc_done)
              prev_v_h = {0: 0, 1: 0}
              prev_t_ys = {0: 0, 1: 0}
              prev_s_st = 0
              prev_v_yc = 0
              for cc in range(NCC):
                  for s in range(DS):
                      pp = s % 2
                      da_buf = tC if pp == 0 else tD
                      wait("scalar", "vc", prev_v_h[pp])
                      em("scalar", lambda e, cc=cc, s=s, b=da_buf: e.activation(
                          out=b[:, 0:SG], in_=bA[:, cc * SG:(cc + 1) * SG],
                          func=AF.Exp,
                          scale=a_sb[:, cc * 16 + s:cc * 16 + s + 1]))
                      s_da = bump("scalar", "sc")

                      def g_w(g, cc=cc, s=s, pp=pp, pv=prev_v_h[pp]):
                          g.wait_ge(vc, pv)
                          g.tensor_tensor(out=wsc[:, pp * SG:(pp + 1) * SG],
                                          in0=u8[:, cc * SG:(cc + 1) * SG],
                                          in1=wbuf[:, s * SG:(s + 1) * SG],
                                          op=AL.mult).then_inc(gc, 1)
                      em("gpsimd", g_w)
                      K["gc"] += 1
                      g_w_c = K["gc"]
                      wait("vector", "sc", s_da)
                      wait("vector", "gc", g_w_c)
                      wait("vector", "sc", prev_s_st)
                      em("vector", lambda e, cc=cc, s=s, pp=pp, b=da_buf:
                         e.tensor_tensor_scan(
                          out=hbuf[:, 0:SG], data0=b[:, 0:SG],
                          data1=wsc[:, pp * SG:(pp + 1) * SG],
                          initial=state[:, cc * 16 + s:cc * 16 + s + 1],
                          op0=AL.mult, op1=AL.add))
                      v_h = bump("vector", "vc")
                      prev_v_h[pp] = v_h
                      wait("scalar", "vc", v_h)
                      em("scalar", lambda e, cc=cc, s=s: e.copy(
                          out=state[:, cc * 16 + s:cc * 16 + s + 1],
                          in_=hbuf[:, SG - 1:SG]))
                      prev_s_st = bump("scalar", "sc")
                      wait("vector", "tc", prev_t_ys[pp])
                      em("vector", lambda e, s=s, pp=pp: e.tensor_mul(
                          out=ycb[:, pp * SG:(pp + 1) * SG], in0=hbuf[:, 0:SG],
                          in1=wbuf[:, 16384 + s * SG:16384 + (s + 1) * SG]))
                      v_yc = bump("vector", "vc")
                      prev_v_yc = v_yc
                      wait("tensor", "vc", v_yc)
                      for jh in range(2):
                          em("tensor", lambda e, s=s, pp=pp, jh=jh: e.matmul(
                              out=ps_y[:, jh * 512:(jh + 1) * 512],
                              lhsT=idb[:, :],
                              rhs=ycb[:, pp * SG + jh * 512:
                                      pp * SG + (jh + 1) * 512],
                              start=(s == 0), stop=False))
                      prev_t_ys[pp] = bump("tensor", "tc")
                  em("vector", lambda e, cc=cc: e.tensor_scalar_mul(
                      out=xd1[:, 0:SG], in0=bB[:, cc * SG:(cc + 1) * SG],
                      scalar1=dv_sb[:, cc:cc + 1]))
                  v_xd = bump("vector", "vc")
                  wait("tensor", "vc", v_xd)
                  for jh in range(2):
                      em("tensor", lambda e, cc=cc, jh=jh: e.matmul(
                          out=ps_y[:, jh * 512:(jh + 1) * 512], lhsT=idb[:, :],
                          rhs=xd1[:, jh * 512:(jh + 1) * 512],
                          start=False, stop=True))
                  t_yd = bump("tensor", "tc")
                  cl2 = dma("sync", bE[:, 0:SG],
                            park_sz[cc * 128:(cc + 1) * 128, tloc:tloc + SG])
                  wait("vector", "tc", t_yd)
                  wait("vector", "sq", cl2)
                  em("vector", lambda e: e.tensor_mul(
                      out=bD[:, 0:SG], in0=ps_y[:, 0:SG], in1=bE[:, 0:SG]))
                  em("vector", lambda e: e.tensor_copy(
                      out=bD[:, SG:2 * SG], in_=ps_y[:, 0:SG]))
                  v_yz = bump("vector", "vc")
                  wait("tensor", "vc", v_yz)
                  wait("sync", "vc", v_yz)

                  def g_yout(g, cc=cc, g2=g2, qown=qown, tloc=tloc, v_yz=v_yz):
                      g.wait_ge(vc, v_yz)
                      g.reg_mov(regs["own"], qown)
                      g.reg_alu(regs["dr"], regs["own"], regs["me_r"],
                                AL.bitwise_xor)
                      g.reg_alu(regs["code"], regs["dr"], 2, AL.mult)
                      g.reg_alu(regs["code"], regs["code"], regs["me_r"], AL.add)
                      dr_v = RuntimeValue(regs["dr"], min_val=0, max_val=1)
                      with g.If(dr_v < 1):
                          g.dma_start(out=park_yz[cc * 128:(cc + 1) * 128,
                                                  tloc:tloc + SG],
                                      in_=bD[:, 0:SG]).then_inc(gq, 16)
                      for code, slot in ((2, 0), (3, 1)):
                          with g.If_eq(regs["code"], code):
                              g.dma_start(out=shy_b[slot * 1024 + cc * 128:
                                                    slot * 1024 + (cc + 1) * 128,
                                                    tloc:tloc + SG],
                                          in_=bD[:, SG:2 * SG]).then_inc(gq, 16)
                  em("gpsimd", g_yout)
                  K["gq"] += 16
                  wait("vector", "gq", K["gq"])

              def g_yflag(g, g2=g2, gq_t=K["gq"]):
                  g.wait_ge(gq, gq_t)
                  g_polldma(g, regs["outbox"][0:1, 8 + g2:9 + g2],
                            fc[1:2, 1:2])
              em("gpsimd", g_yflag)
              barrier()

        if KSTOP >= 5:
          wtrans(wo_n, DM, DI, 1024)
          c_sel4 = dma("sync", sel4f, sel4)
          wait("tensor", "sq", c_sel4)
          barrier()
          for tau in range(NT):
              def g_ypoll(g, tau=tau):
                  g.reg_mov(regs["fv"], 0)
                  fv_v = RuntimeValue(regs["fv"], min_val=0, max_val=4)
                  with g.While(fv_v < 1):
                      g_polldma(g, fb[0:1, 0:64], regs["inbox"])
                      g.reg_load(regs["own"], fb[0:1, 8 + tau // 2:9 + tau // 2])
                      g.reg_load(regs["dr"], fb[0:1, 10 + tau // 2:11 + tau // 2])
                      g.reg_alu(regs["dr"], regs["dr"], regs["me_r"], AL.mult)
                      g.reg_alu(regs["gidx"], regs["me_r"], 1, AL.bitwise_xor)
                      g.reg_alu(regs["own"], regs["own"], regs["gidx"], AL.mult)
                      g.reg_alu(regs["fv"], regs["own"], regs["dr"], AL.add)
                  for slot in (0, 1):
                      with g.If_eq(regs["me_r"], 1 - slot):
                          for cc in range(8):
                              g.dma_start(
                                  out=bD[:, (8 + cc) * TT:(9 + cc) * TT],
                                  in_=shy_b[slot * 1024 + cc * 128:
                                            slot * 1024 + (cc + 1) * 128,
                                            tau * TT:(tau + 1) * TT]).then_inc(gq, 16)
              em("gpsimd", g_ypoll)
              K["gq"] += 16 * 8
              gq_y = K["gq"]
              cl = 0
              for cc in range(8):
                  cl = dma("sync", bD[:, cc * TT:(cc + 1) * TT],
                           park_yz[cc * 128:(cc + 1) * 128,
                                   tau * TT:(tau + 1) * TT])
              v_prev = 0
              for cc in range(8, 16):
                  c_z = dma("sync", bE[:, 0:TT],
                            park_sz[cc * 128:(cc + 1) * 128,
                                    tau * TT:(tau + 1) * TT])
                  wait("vector", "sq", c_z)
                  wait("vector", "gq", gq_y)
                  em("vector", lambda e, cc=cc: e.tensor_mul(
                      out=bD[:, cc * TT:(cc + 1) * TT],
                      in0=bD[:, cc * TT:(cc + 1) * TT], in1=bE[:, 0:TT]))
                  v_prev = bump("vector", "vc")
                  wait("sync", "vc", v_prev)
              wait("tensor", "sq", cl)
              wait("tensor", "vc", v_prev)
              clx = 0
              for dc in range(8):
                  clx = dma("scalar", tA[:, dc * TT:(dc + 1) * TT],
                            park_x[dc * 128:(dc + 1) * 128,
                                   HW + tau * TT:HW + (tau + 1) * TT])
              for dc in range(8):
                  bank = ps_a if (dc % 2 == 0) else ps_b
                  for kc in range(16):
                      em("tensor", lambda e, dc=dc, kc=kc, bank=bank: e.matmul(
                          out=bank[:, 0:TT],
                          lhsT=wbuf[:, kc * 1024 + dc * 128:
                                    kc * 1024 + dc * 128 + 128],
                          rhs=bD[:, kc * TT:(kc + 1) * TT],
                          start=(kc == 0), stop=(kc == 15)))
                  t_mm = bump("tensor", "tc")
                  wait("vector", "tc", t_mm)
                  wait("vector", "cq", clx)
                  em("vector", lambda e, dc=dc, bank=bank: e.tensor_add(
                      out=tA[:, dc * TT:(dc + 1) * TT],
                      in0=tA[:, dc * TT:(dc + 1) * TT], in1=bank[:, 0:TT]))
                  v_r = bump("vector", "vc")
                  wait("tensor", "vc", v_r)
              v_xp = K["vc"]
              wait("sync", "vc", v_xp)
              for dc in range(8):
                  dma("sync", park_xp[dc * 128:(dc + 1) * 128,
                                      tau * TT:(tau + 1) * TT],
                      tA[:, dc * TT:(dc + 1) * TT])
              v_ln = layernorm(tA, bD, tB, TT)
              s_mx = token_max_bcast(tB, 8, TT, tD)
              wait("vector", "sc", s_mx)
              em("vector", lambda e: e.tensor_scalar_mul(
                  out=isx_t[:, 0:TT], in0=tD[:, 0:TT], scalar1=1.0 / 127.0))
              v_is = bump("vector", "vc")
              wait("sync", "vc", v_is)
              c_is = dma("sync", park_isx[:, tau * TT:(tau + 1) * TT],
                         isx_t[:, 0:TT])
              em("vector", lambda e: e.reciprocal(out=tC[:, 0:TT],
                                                  in_=tD[:, 0:TT]))
              vsync()
              em("vector", lambda e: e.tensor_mul(
                  out=tD[:, 512:512 + TT], in0=tD[:, 0:TT], in1=tC[:, 0:TT]))
              vsync()
              em("vector", lambda e: e.tensor_scalar(
                  out=tD[:, 512:512 + TT], in0=tD[:, 512:512 + TT],
                  scalar1=2.0, scalar2=None, op0=AL.subtract))
              vsync()
              em("vector", lambda e: e.tensor_mul(
                  out=tC[:, 0:TT], in0=tC[:, 0:TT], in1=tD[:, 512:512 + TT]))
              vsync()
              em("vector", lambda e: e.tensor_scalar_mul(
                  out=tC[:, 0:TT], in0=tC[:, 0:TT], scalar1=-127.0))
              vsync()
              v_rs = bump("vector", "vc")
              for kc in range(8):
                  em("vector", lambda e, kc=kc: e.tensor_mul(
                      out=tD[:, 0:TT], in0=tB[:, kc * TT:(kc + 1) * TT],
                      in1=tC[:, 0:TT]))
                  em("vector", lambda e, kc=kc: e.tensor_scalar(
                      out=bD[:, kc * TT:(kc + 1) * TT], in0=tD[:, 0:TT],
                      scalar1=MAGIC, scalar2=MAGIC, op0=AL.add, op1=AL.subtract))
              v_xq = bump("vector", "vc")
              wait("sync", "vc", v_xq)
              for kc in range(8):
                  dma("sync", park_xq[kc * 128:(kc + 1) * 128,
                                      tau * TT:(tau + 1) * TT],
                      bD[:, kc * TT:(kc + 1) * TT])
              barrier()

        if KSTOP >= 6:
          def wq_scale2(w_src, rows, cols, scol):
              n_ec = (rows + 127) // 128
              em("vector", lambda e: e.memset(tD[:, 0:4], 0.0))
              bump("vector", "vc")
              nch = 0
              for ec in range(n_ec):
                  r0 = ec * 128
                  pr = min(128, rows - r0)
                  for c0 in range(0, cols, 2048):
                      cw_ = min(2048, cols - c0)
                      c_l = dma("scalar", wnat[0:pr, 0:cw_],
                                w_src[r0:r0 + pr, c0:c0 + cw_])
                      wait("vector", "cq", c_l)
                      em("vector", lambda e, pr=pr, cw_=cw_, nch=nch:
                         e.tensor_reduce(
                          out=tD[0:pr, 8 + nch:9 + nch],
                          in_=wnat[0:pr, 0:cw_],
                          axis=mybir.AxisListType.X, op=AL.add,
                          apply_absolute_value=True))
                      nch += 1
                      v_c = bump("vector", "vc")
                      wait("scalar", "vc", v_c)
              vsync()
              em("vector", lambda e, nch=nch: e.tensor_reduce(
                  out=tD[:, 0:1], in_=tD[:, 8:8 + nch],
                  axis=mybir.AxisListType.X, op=AL.add))
              v_acc = bump("vector", "vc")
              wait("tensor", "vc", v_acc)
              em("tensor", lambda e: e.transpose(
                  out=ps_s[0:1, 0:128], in_=tD[:, 0:1], identity=idf[:, :]))
              t_t = bump("tensor", "tc")
              wait("vector", "tc", t_t)
              em("vector", lambda e: e.tensor_reduce(
                  out=tD[0:1, 1:2], in_=ps_s[0:1, 0:128],
                  axis=mybir.AxisListType.X, op=AL.add))
              n = float(rows * cols)
              vsync()
              em("vector", lambda e, n=n: e.tensor_scalar_mul(
                  out=tD[0:1, 2:3], in0=tD[0:1, 1:2], scalar1=1.0 / n))
              vsync()
              em("vector", lambda e: e.tensor_scalar_max(
                  out=tD[0:1, 2:3], in0=tD[0:1, 2:3], scalar1=1e-5))
              vsync()
              em("vector", lambda e: e.reciprocal(out=tD[0:1, 3:4],
                                                  in_=tD[0:1, 2:3]))
              vsync()
              em("vector", lambda e: e.tensor_tensor(
                  out=tD[0:1, 4:5], in0=tD[0:1, 2:3], in1=tD[0:1, 3:4],
                  op=AL.mult))
              vsync()
              em("vector", lambda e: e.tensor_scalar(
                  out=tD[0:1, 4:5], in0=tD[0:1, 4:5], scalar1=2.0,
                  scalar2=None, op0=AL.subtract))
              vsync()
              em("vector", lambda e: e.tensor_tensor(
                  out=tD[0:1, 4:5], in0=tD[0:1, 4:5], in1=tD[0:1, 3:4],
                  op=AL.mult))
              vsync()
              em("vector", lambda e: e.tensor_scalar_mul(
                  out=tD[0:1, 3:4], in0=tD[0:1, 4:5], scalar1=-1.0))
              v_s = bump("vector", "vc")
              wait("tensor", "vc", v_s)
              em("tensor", lambda e: e.matmul(
                  out=ps_s[0:128, 0:2], lhsT=onerf[0:1, :], rhs=tD[0:1, 2:4],
                  start=True, stop=True))
              t_b = bump("tensor", "tc")
              wait("scalar", "tc", t_b)
              em("scalar", lambda e, scol=scol: e.copy(
                  out=wsg[:, scol:scol + 2], in_=ps_s[:, 0:2]))
              return ("sc", bump("scalar", "sc"))

        if KSTOP >= 6:
          dep = wq_scale2(wg_n, FF, DM, 0)
          wait("vector", dep[0], dep[1])
          if os.environ.get("KW4"):
              barrier()
        if KSTOP >= 6 and not os.environ.get("KW4"):
          wtrans(wg_n, FF, DM, 4096, quant=1)
          barrier()
          for tau in range(NT):
              cl = 0
              for kc in range(8):
                  cl = dma("sync", bD[:, kc * TT:(kc + 1) * TT],
                           park_xq[kc * 128:(kc + 1) * 128,
                                   tau * TT:(tau + 1) * TT])
              c_is = dma("sync", isx_t[:, 0:TT],
                         park_isx[:, tau * TT:(tau + 1) * TT])
              wait("vector", "sq", c_is)
              wait("tensor", "sq", cl)
              s_prev = 0
              for ec in range(32):
                  bank = ps_a if (ec % 2 == 0) else ps_b
                  for kc in range(8):
                      em("tensor", lambda e, ec=ec, kc=kc, bank=bank: e.matmul(
                          out=bank[:, 0:TT],
                          lhsT=wbuf[:, kc * 4096 + ec * 128:
                                    kc * 4096 + ec * 128 + 128],
                          rhs=bD[:, kc * TT:(kc + 1) * TT],
                          start=(kc == 0), stop=(kc == 7)))
                  t_mm = bump("tensor", "tc")
                  wait("vector", "tc", t_mm)
                  wait("vector", "sc", s_prev)
                  em("vector", lambda e, bank=bank: e.scalar_tensor_tensor(
                      out=tC[:, 0:TT], in0=bank[:, 0:TT], scalar=wsg[:, 0:1],
                      in1=isx_t[:, 0:TT], op0=AL.mult, op1=AL.mult))
                  v_s = bump("vector", "vc")
                  wait("tensor", "vc", v_s)
                  wait("scalar", "vc", v_s)
                  em("scalar", lambda e: e.activation(out=bE[:, 0:TT],
                                                      in_=tC[:, 0:TT],
                                                      func=AF.Silu))
                  s_g = bump("scalar", "sc")
                  wait("sync", "sc", s_g)
                  c_s = dma("sync", park_g[ec * 128:(ec + 1) * 128,
                                           tau * TT:(tau + 1) * TT],
                            bE[:, 0:TT])
                  s_prev = s_g
                  wait("scalar", "sq", c_s)
              barrier()

        if KSTOP >= 7:
          dep = wq_scale2(wu_n, FF, DM, 2)
          wait("vector", dep[0], dep[1])
          wtrans(wu_n, FF, DM, 4096, quant=3)
          barrier()
          for tau in range(NT):
              cl = 0
              for kc in range(8):
                  cl = dma("sync", bD[:, kc * TT:(kc + 1) * TT],
                           park_xq[kc * 128:(kc + 1) * 128,
                                   tau * TT:(tau + 1) * TT])
              c_is = dma("sync", isx_t[:, 0:TT],
                         park_isx[:, tau * TT:(tau + 1) * TT])
              wait("vector", "sq", c_is)
              wait("tensor", "sq", cl)
              em("vector", lambda e: e.memset(tD[:, 512:512 + TT], 0.0))
              bump("vector", "vc")
              for ec in range(32):
                  bank = ps_a if (ec % 2 == 0) else ps_b
                  for kc in range(8):
                      em("tensor", lambda e, ec=ec, kc=kc, bank=bank: e.matmul(
                          out=bank[:, 0:TT],
                          lhsT=wbuf[:, kc * 4096 + ec * 128:
                                    kc * 4096 + ec * 128 + 128],
                          rhs=bD[:, kc * TT:(kc + 1) * TT],
                          start=(kc == 0), stop=(kc == 7)))
                  t_mm = bump("tensor", "tc")
                  c_g = dma("scalar", bE[:, 0:TT],
                            park_g[ec * 128:(ec + 1) * 128,
                                   tau * TT:(tau + 1) * TT])
                  wait("vector", "tc", t_mm)
                  wait("vector", "cq", c_g)
                  em("vector", lambda e, bank=bank:
                     e.scalar_tensor_tensor(
                      out=tC[:, 0:TT], in0=bank[:, 0:TT], scalar=wsg[:, 2:3],
                      in1=isx_t[:, 0:TT], op0=AL.mult, op1=AL.mult))
                  em("vector", lambda e: e.tensor_mul(
                      out=tC[:, 512:512 + TT], in0=tC[:, 0:TT], in1=bE[:, 0:TT]))
                  v_gm = bump("vector", "vc")
                  wait("scalar", "vc", v_gm)
                  em("scalar", lambda e: e.activation(
                      out=tD[:, 0:TT], in_=tC[:, 512:512 + TT], func=AF.Abs))
                  s_ga = bump("scalar", "sc")
                  wait("vector", "sc", s_ga)
                  em("vector", lambda e: e.tensor_max(
                      out=tD[:, 512:512 + TT], in0=tD[:, 512:512 + TT],
                      in1=tD[:, 0:TT]))
                  v_g = bump("vector", "vc")
                  wait("tensor", "vc", v_g)
                  wait("scalar", "vc", v_g)
                  wait("sync", "vc", v_g)
                  c_p = dma("sync", park_guf[ec * 128:(ec + 1) * 128,
                                             tau * TT:(tau + 1) * TT],
                            tC[:, 512:512 + TT])
                  wait("vector", "sq", c_p)
              em("vector", lambda e: e.tensor_copy(out=tC[:, 0:TT],
                                                   in_=tD[:, 512:512 + TT]))
              v_mx = bump("vector", "vc")
              wait("tensor", "vc", v_mx)
              for j in range(4):
                  em("tensor", lambda e, j=j: e.transpose(
                      out=ps_t[0:128, j * 128:(j + 1) * 128],
                      in_=tC[:, j * 128:(j + 1) * 128], identity=idf[:, :]))
              t_t = bump("tensor", "tc")
              wait("vector", "tc", t_t)
              for j in range(4):
                  em("vector", lambda e, j=j: e.tensor_reduce(
                      out=tD[:, j:j + 1], in_=ps_t[:, j * 128:(j + 1) * 128],
                      axis=mybir.AxisListType.X, op=AL.max))
              v_r = bump("vector", "vc")
              wait("tensor", "vc", v_r)
              em("tensor", lambda e: e.transpose(
                  out=ps_s[0:4, 0:128], in_=tD[:, 0:4], identity=idf[:, :]))
              t_t2 = bump("tensor", "tc")
              wait("scalar", "tc", t_t2)
              em("scalar", lambda e: e.copy(out=tD[0:4, 4:132],
                                            in_=ps_s[0:4, 0:128]))
              s_c = bump("scalar", "sc")
              wait("vector", "sc", s_c)
              em("vector", lambda e: e.tensor_scalar_max(
                  out=tD[0:4, 4:132], in0=tD[0:4, 4:132], scalar1=1e-5))
              v_b = bump("vector", "vc")
              wait("tensor", "vc", v_b)
              for j in range(4):
                  em("tensor", lambda e, j=j: e.matmul(
                      out=ps_s[0:128, j * 128:(j + 1) * 128],
                      lhsT=sel4f[0:4, j * 128:(j + 1) * 128],
                      rhs=tD[0:4, 4:132], start=True, stop=True))
              t_b5 = bump("tensor", "tc")
              wait("scalar", "tc", t_b5)
              em("scalar", lambda e: e.copy(out=tD[:, 0:TT], in_=ps_s[:, 0:TT]))
              s_b5 = bump("scalar", "sc")
              wait("vector", "sc", s_b5)
              em("vector", lambda e: e.tensor_scalar_mul(
                  out=is2_t[:, 0:TT], in0=tD[:, 0:TT], scalar1=1.0 / 127.0))
              v_i2 = bump("vector", "vc")
              wait("sync", "vc", v_i2)
              c_i2 = dma("sync", park_is2[:, tau * TT:(tau + 1) * TT],
                         is2_t[:, 0:TT])
              em("vector", lambda e: e.reciprocal(out=tC[:, 0:TT],
                                                  in_=tD[:, 0:TT]))
              vsync()
              em("vector", lambda e: e.tensor_mul(
                  out=tD[:, 512:512 + TT], in0=tD[:, 0:TT], in1=tC[:, 0:TT]))
              vsync()
              em("vector", lambda e: e.tensor_scalar(
                  out=tD[:, 512:512 + TT], in0=tD[:, 512:512 + TT],
                  scalar1=2.0, scalar2=None, op0=AL.subtract))
              vsync()
              em("vector", lambda e: e.tensor_mul(
                  out=tC[:, 0:TT], in0=tC[:, 0:TT], in1=tD[:, 512:512 + TT]))
              vsync()
              em("vector", lambda e: e.tensor_scalar_mul(
                  out=tC[:, 0:TT], in0=tC[:, 0:TT], scalar1=-127.0))
              vsync()
              v_rs = bump("vector", "vc")
              wait("scalar", "vc", v_rs)
              for ec in range(32):
                  c_l = dma("scalar", tD[:, 512:512 + TT],
                            park_guf[ec * 128:(ec + 1) * 128,
                                     tau * TT:(tau + 1) * TT])
                  wait("vector", "cq", c_l)
                  em("vector", lambda e: e.tensor_mul(
                      out=tD[:, 0:TT], in0=tD[:, 512:512 + TT], in1=tC[:, 0:TT]))
                  em("vector", lambda e, ec=ec: e.tensor_scalar(
                      out=bE[:, 0:TT], in0=tD[:, 0:TT],
                      scalar1=MAGIC, scalar2=MAGIC, op0=AL.add, op1=AL.subtract))
                  v_q = bump("vector", "vc")
                  wait("sync", "vc", v_q)
                  c_s = dma("sync", park_guq[ec * 128:(ec + 1) * 128,
                                             tau * TT:(tau + 1) * TT],
                            bE[:, 0:TT])
                  wait("vector", "sq", c_s)
                  wait("scalar", "vc", v_q)
              barrier()

        final_store = 0
        if KSTOP >= 8:
          dep = wq_scale2(wdn_n, DM, FF, 4)
          wait("vector", dep[0], dep[1])
          wtrans(wdn_n, DM, FF, 1024, quant=5)
          barrier()
          for tau in range(NT):
              cl = 0
              for kc in range(16):
                  cl = dma("sync", bD[:, kc * TT:(kc + 1) * TT],
                           park_guq[kc * 128:(kc + 1) * 128,
                                    tau * TT:(tau + 1) * TT])
              for kc in range(16, 32):
                  cl = dma("sync", bB[:, (kc - 16) * TT:(kc - 15) * TT],
                           park_guq[kc * 128:(kc + 1) * 128,
                                    tau * TT:(tau + 1) * TT])
              clx = 0
              for dc in range(8):
                  clx = dma("scalar", tA[:, dc * TT:(dc + 1) * TT],
                            park_xp[dc * 128:(dc + 1) * 128,
                                    tau * TT:(tau + 1) * TT])
              c_i2 = dma("sync", is2_t[:, 0:TT],
                         park_is2[:, tau * TT:(tau + 1) * TT])
              wait("vector", "sq", c_i2)
              wait("tensor", "sq", cl)
              for dc in range(8):
                  bank = ps_a if (dc % 2 == 0) else ps_b
                  for kc in range(32):
                      src = bD if kc < 16 else bB
                      koff = (kc if kc < 16 else kc - 16) * TT
                      em("tensor", lambda e, dc=dc, kc=kc, bank=bank, src=src,
                         koff=koff: e.matmul(
                          out=bank[:, 0:TT],
                          lhsT=wbuf[:, kc * 1024 + dc * 128:
                                    kc * 1024 + dc * 128 + 128],
                          rhs=src[:, koff:koff + TT],
                          start=(kc == 0), stop=(kc == 31)))
                  t_mm = bump("tensor", "tc")
                  wait("vector", "tc", t_mm)
                  wait("vector", "cq", clx)
                  em("vector", lambda e, bank=bank, tau=tau:
                     e.scalar_tensor_tensor(
                      out=tC[:, 0:TT], in0=bank[:, 0:TT], scalar=wsg[:, 4:5],
                      in1=is2_t[:, 0:TT], op0=AL.mult, op1=AL.mult))
                  em("vector", lambda e, dc=dc: e.tensor_add(
                      out=tA[:, dc * TT:(dc + 1) * TT],
                      in0=tA[:, dc * TT:(dc + 1) * TT], in1=tC[:, 0:TT]))
                  v_o = bump("vector", "vc")
                  wait("tensor", "vc", v_o)
              v_out = K["vc"]
              wait("tensor", "vc", v_out)
              for tsub in range(4):
                  for d4 in range(0, 8, 4):
                      for j in range(4):
                          dc = d4 + j
                          em("tensor", lambda e, j=j, dc=dc, tsub=tsub:
                             e.transpose(
                              out=ps_t[0:128, j * 128:(j + 1) * 128],
                              in_=tA[:, dc * TT + tsub * 128:
                                     dc * TT + tsub * 128 + 128],
                              identity=idf[:, :]))
                      t_c = bump("tensor", "tc")
                      wait("scalar", "tc", t_c)
                      for j in range(4):
                          dc = d4 + j
                          em("scalar", lambda e, j=j, dc=dc: e.copy(
                              out=tB[:, dc * 128:(dc + 1) * 128],
                              in_=ps_t[:, j * 128:(j + 1) * 128]))
                      s_c = bump("scalar", "sc")
                      wait("tensor", "sc", s_c)
                  s_row = K["sc"]
                  wait("sync", "sc", s_row)
                  final_store = dma(
                      "sync", out_h[tau * TT + tsub * 128:
                                    tau * TT + tsub * 128 + 128, :],
                      tB[:, 0:DM])
                  wait("scalar", "sq", final_store)
        if KDUMP == "wbuf":
            dstb = out_h.bitcast(BF16).rearrange("a b -> (a b)")
            for i in range(16):
                final_store = dma(
                    "sync", dstb[i * 128 * 2048:(i + 1) * 128 * 2048]
                    .rearrange("(a b) -> a b", b=2048),
                    wbuf[:, i * 2048:(i + 1) * 2048])
        elif KDUMP == "wsg":
            final_store = dma("sync", out_h[0:128, 0:8], wsg[:, 0:8])
        elif KDUMP == "td":
            final_store = dma("sync", out_h[0:128, 0:512], tD[:, 0:512])
        elif KDUMP:
            parks = {"park_sz": (park_sz, BF16), "park_xbf": (park_xbf, BF16),
                     "park_din": (park_din, BF16), "park_bcl": (park_bcl, F32),
                     "park_dl": (park_dl, BF16), "park_yz": (park_yz, BF16),
                     "park_xp": (park_xp, F32), "park_x": (park_x, F32),
                     "park_g": (park_g, BF16), "park_guf": (park_guf, F32),
                     "park_guq": (park_guq, BF16), "park_xq": (park_xq, BF16),
                     "park_isx": (park_isx, F32), "park_is2": (park_is2, F32),
                     "sh_d": (shd_b, BF16), "sh_xb": (shxb_b, BF16),
                     "sh_bc": (shbc_f, F32), "sh_y": (shy_b, BF16)}
            pk, dt_ = parks[KDUMP]
            rows, cols = pk.shape
            dst = out_h.bitcast(BF16) if dt_ == BF16 else out_h
            drows, dcols = dst.shape
            n = min(rows * cols, drows * dcols)
            per = cols
            nr = min(rows, n // per)
            for r0 in range(0, nr, 128):
                rr = min(128, nr - r0)
                final_store = dma("sync",
                                  dst.rearrange("a b -> (a b)")[r0 * per:(r0 + rr) * per]
                                  .rearrange("(a b) -> a b", b=per),
                                  pk[r0:r0 + rr, :])
        wait("sync", "sq", final_store)
        barrier()

        with nc.Block() as block:
            @block.sync
            def _(e):
                for f in prog["sync"]:
                    f(e)

            @block.gpsimd
            def _(e):
                for f in prog["gpsimd"]:
                    f(e)

            @block.vector
            def _(e):
                for f in prog["vector"]:
                    f(e)

            @block.scalar
            def _(e):
                for f in prog["scalar"]:
                    f(e)

            @block.tensor
            def _(e):
                for f in prog["tensor"]:
                    f(e)

    return nc


def kernel(L0_attn_x, L0_attn_in_proj_w, L0_attn_conv_w, L0_attn_conv_b,
           L0_attn_x_proj_w, L0_attn_dt_proj_w, L0_attn_dt_proj_b,
           L0_attn_A_log, L0_attn_D, L0_attn_out_proj_w,
           L0_ffn_w_gate, L0_ffn_w_up, L0_ffn_w_down):
    x = np.ascontiguousarray(np.asarray(L0_attn_x, dtype=np.float32))
    B, L, _ = x.shape
    in_maps = []
    eye_f = np.eye(128, dtype=np.float32)
    eye_b = np.eye(128, dtype=ml_dtypes.bfloat16)
    ones_c = np.ones((128, 1), dtype=ml_dtypes.bfloat16)
    ones_r = np.ones((1, 128), dtype=ml_dtypes.bfloat16)
    fcv = np.stack([np.zeros(64, np.int32), np.ones(64, np.int32)])
    sel_bc = np.zeros((32, 4096), dtype=ml_dtypes.bfloat16)
    for s in range(32):
        sel_bc[s, s * 128:(s + 1) * 128] = 1
    sel_4 = np.zeros((4, 512), np.float32)
    for j in range(4):
        sel_4[j, j * 128:(j + 1) * 128] = 1
    wi = np.asarray(L0_attn_in_proj_w, np.float32)
    wx = np.asarray(L0_attn_x_proj_w, np.float32)
    wd = np.asarray(L0_attn_dt_proj_w, np.float32)
    wo = np.asarray(L0_attn_out_proj_w, np.float32)
    cw = np.asarray(L0_attn_conv_w, np.float32)[:, 0, :]
    cb = np.asarray(L0_attn_conv_b, np.float32)
    db = np.asarray(L0_attn_dt_proj_b, np.float32)
    alog = np.asarray(L0_attn_A_log, np.float32)
    dvec = np.asarray(L0_attn_D, np.float32)
    wg = np.asarray(L0_ffn_w_gate, np.float32)
    wu = np.asarray(L0_ffn_w_up, np.float32)
    wdn = np.asarray(L0_ffn_w_down, np.float32)

    for c in range(8):
        b, P = c >> 1, c & 1
        perm = np.arange(DI) if P == 0 else np.concatenate(
            [np.arange(1024, 2048), np.arange(0, 1024)])
        g0 = P * TL
        if g0 >= HW:
            xe = x[b, g0 - HW:g0 + TL, :]
        else:
            xe = np.concatenate(
                [np.zeros((HW - g0, DM), np.float32), x[b, 0:g0 + TL, :]], 0)
        wi_p = np.concatenate([wi[0:DI][perm], wi[DI:2 * DI][perm]], 0)
        wx_p = wx[:, perm]
        wd_p = wd[perm, :]
        wo_p = wo[:, perm]
        cw_p = cw[perm].reshape(16, 128, 4).transpose(1, 0, 2).reshape(128, 64)
        cb_p = cb[perm].reshape(16, 128).T
        db_p = db[perm].reshape(16, 128).T
        al_p = alog[perm[0:1024]].reshape(8, 128, 16).transpose(1, 0, 2)\
            .reshape(128, 128)
        dv_p = dvec[perm[0:1024]].reshape(8, 128).T
        in_maps.append({
            "x_ext": np.ascontiguousarray(xe),
            "wi_n": np.ascontiguousarray(wi_p),
            "wx_n": np.ascontiguousarray(wx_p),
            "wd_n": np.ascontiguousarray(wd_p),
            "wo_n": np.ascontiguousarray(wo_p),
            "wg_n": np.ascontiguousarray(wg),
            "wu_n": np.ascontiguousarray(wu),
            "wdn_n": np.ascontiguousarray(wdn),
            "cwr": np.ascontiguousarray(cw_p),
            "cbr": np.ascontiguousarray(cb_p),
            "dbr": np.ascontiguousarray(db_p),
            "alogr": np.ascontiguousarray(al_p),
            "dr": np.ascontiguousarray(dv_p),
            "identf": eye_f, "identb": eye_b,
            "onescol": ones_c, "onesrow": ones_r,
            "onesrowf": np.ones((1, 128), np.float32),
            "cvals": np.tile(np.array([[1e-5, 1.0, -1.0, 0.0]], np.float32),
                             (128, 1)),
            "selbc": sel_bc,
            "sel4": sel_4,
            "fconst": fcv,
        })
    nc = build_nc()
    res = run_bass_kernel_spmd(nc, in_maps, core_ids=list(range(8)))
    out = np.empty((B, L, DM), np.float32)
    for c in range(8):
        b, P = c >> 1, c & 1
        out[b, P * TL:(P + 1) * TL, :] = res.results[c]["o"]
    return out


if __name__ == "__main__":
    import reference as R
    import jax
    cpu = jax.local_devices(backend="cpu")[0]
    with jax.default_device(cpu):
        inputs = {k: np.asarray(v) for k, v in R.setup_inputs(0).items()}
    out = kernel(**inputs)
    with jax.default_device(cpu):
        import jax.numpy as jnp
        ref = np.asarray(R.reference(**{k: jnp.asarray(v)
                                        for k, v in inputs.items()}))
    err = out - ref
    rv = (err ** 2).mean() / (ref.astype(np.float64) ** 2).mean()
    print("rvr:", rv, "maxabs:", np.abs(err).max())
